# version 24
# baseline (speedup 1.0000x reference)
import numpy as np
from contextlib import ExitStack

import concourse.bass as bass
import concourse.mybir as mybir
from concourse.bass_utils import run_bass_kernel_spmd

F32 = mybir.dt.float32
BF16 = mybir.dt.bfloat16
AF = mybir.ActivationFunctionType
ALU = mybir.AluOpType
AP = bass.AP

B, T, C = 4, 4096, 1024
TC = 2048
TILE = 512
NT = TC // TILE
NPT = TC // TILE
EPS_RMS = float(np.finfo(np.float32).eps)
EPS_LN = 1e-5
INV_C = float(np.float32(1.0 / 1024.0))
INV_DH = float(np.float32(1.0 / 128.0))

DEBUG_TAPS = ()



def qkv_pack_perm():
    perm = np.zeros(2048, np.int64)
    for g in range(4):
        for tau in range(4):
            cch = 4 * g + tau
            for r in range(128):
                h = 4 * g + r // 32
                i0 = r % 32
                i = (i0, 64 + i0, 32 + i0, 96 + i0)[tau]
                perm[cch * 128 + r] = h * 128 + i
    return perm


def f_of_g():
    g = np.arange(4096)
    return (g % 32) * 128 + (g // 32)


def rope_tables(t0):
    freq = (1.0 / 1024.0) ** np.linspace(0.0, 1.0, 32)
    r = np.arange(128)
    fr = freq[r % 32]
    t = np.arange(2048) + t0
    th = np.outer(fr, t)
    return np.cos(th).astype(np.float32), np.sin(th).astype(np.float32)


def stage_mat_f(w, s):
    sig = 1 << s
    M = np.zeros((32, 128, 128), np.float32)
    rr = np.arange(128)
    for cch in range(32):
        f = cch * 128 + rr
        blk = f // (2 * sig)
        j = f % sig
        half = (f // sig) % 2
        wij = w[blk * sig + j]
        M[cch, rr, rr] = wij[rr, half, half]
        M[cch, rr, (f ^ sig) - cch * 128] = wij[rr, half, 1 - half]
    return M


def stage_mat_g(w, s):
    sig = 1 << s
    fg = f_of_g()
    M = np.zeros((32, 128, 128), np.float32)
    rr = np.arange(128)
    for gc in range(32):
        g = gc * 128 + rr
        f = fg[g]
        fp = f ^ sig
        gpart = (fp % 128) * 32 + (fp // 128)
        assert np.all(gpart // 128 == gc)
        blk = f // (2 * sig)
        j = f % sig
        half = (f // sig) % 2
        wij = w[blk * sig + j]
        M[gc, rr, rr] = wij[rr, half, half]
        M[gc, rr, gpart % 128] = wij[rr, half, 1 - half]
    return M


def build_in_maps(inputs):
    x = np.asarray(inputs["L0_attn_x"], np.float32)
    perm = qkv_pack_perm()
    fg = f_of_g()

    w2T = np.asarray(inputs["L0_attn_qkv_w2"], np.float32).T
    w2T_packed = np.concatenate(
        [w2T[:, perm], w2T[:, 2048 + perm], w2T[:, 4096 + perm]], axis=1)
    w1T = np.asarray(inputs["L0_attn_qkv_w1"], np.float32).T
    cw1T = np.asarray(inputs["L0_attn_cproj_w1"], np.float32).T[perm, :]
    cw2T = np.asarray(inputs["L0_attn_cproj_w2"], np.float32).T
    wexpT = np.asarray(inputs["L0_ffn_W_exp"], np.float32).T.copy()
    wconG = np.asarray(inputs["L0_ffn_W_con"], np.float32)[fg, :].copy()

    upw = np.asarray(inputs["L0_ffn_up_weights"], np.float32)
    dnw = np.asarray(inputs["L0_ffn_down_weights"], np.float32)

    scU = np.zeros((17, 32, 128, 128), np.float32)
    for s in range(6):
        scU[s] = stage_mat_f(upw[s], s)
    scU[6] = stage_mat_f(upw[6], 6).transpose(0, 2, 1)
    for s in range(7, 11):
        scU[s] = stage_mat_g(upw[s], s)
    scU[11] = stage_mat_g(upw[11], 11).transpose(0, 2, 1)
    scU[12] = scU[1].transpose(0, 2, 1)
    scU[13] = scU[3].transpose(0, 2, 1)
    scU[14] = scU[5].transpose(0, 2, 1)
    scU[15] = stage_mat_g(upw[8], 8).transpose(0, 2, 1)
    scU[16] = stage_mat_g(upw[10], 10).transpose(0, 2, 1)

    scD = np.zeros((17, 32, 128, 128), np.float32)
    scD[0] = stage_mat_f(dnw[0], 0)
    for s in range(1, 7):
        scD[s] = stage_mat_f(dnw[s], s).transpose(0, 2, 1)
    scD[7] = stage_mat_g(dnw[7], 7)
    for s in range(8, 12):
        scD[s] = stage_mat_g(dnw[s], s).transpose(0, 2, 1)
    scD[12] = stage_mat_f(dnw[1], 1)
    scD[13] = stage_mat_f(dnw[3], 3)
    scD[14] = stage_mat_f(dnw[5], 5)
    scD[15] = stage_mat_g(dnw[8], 8)
    scD[16] = stage_mat_g(dnw[10], 10)

    bexp_sq = np.asarray(inputs["L0_ffn_b_exp"], np.float32).reshape(32, 128).T.copy()
    ubias_col = np.asarray(inputs["L0_ffn_up_bias"], np.float32)[fg] \
        .reshape(32, 128).T.copy()
    cbias_col = np.asarray(inputs["L0_ffn_b_con"], np.float32).reshape(8, 128).T.copy()
    dbias_col = np.asarray(inputs["L0_ffn_down_bias"], np.float32) \
        .reshape(8, 128).T.copy()

    selSum = np.zeros((8, 128, 48), np.float32)
    Ehead = np.zeros((8, 128, 128), np.float32)
    r = np.arange(128)
    for g in range(4):
        selSum[g, r, 4 * g + r // 32] = 1.0
        selSum[4 + g, r, 32 + 4 * g + r // 32] = 1.0
        Ehead[g, 4 * g + r // 32, r] = 1.0
        Ehead[4 + g, 32 + 4 * g + r // 32, r] = 1.0
    E01 = np.zeros((128, 128), np.float32)
    E01[0, :] = 1.0
    E01[32, :] = 1.0
    E23 = np.zeros((128, 128), np.float32)
    E23[64, :] = 1.0
    E23[96, :] = 1.0
    consts = np.zeros((128, 4), np.float32)
    consts[:, 0] = EPS_RMS
    consts[:, 1] = EPS_LN
    consts[:, 2] = INV_C
    consts[:, 3] = INV_DH

    ropeC0, ropeS0 = rope_tables(0)
    ropeC1, ropeS1 = rope_tables(2048)

    shared = dict(
        w1T=w1T, w2T=w2T_packed, cw1T=cw1T, cw2T=cw2T,
        wexpT=wexpT, wconG=wconG, scU=scU, scD=scD,
        bexp_sq=bexp_sq, ubias_col=ubias_col, cbias_col=cbias_col,
        dbias_col=dbias_col, selSum=selSum, Ehead=Ehead, E01=E01, E23=E23,
        ones_col=np.ones((128, 1), np.float32),
        ones_tile=np.ones((128, TILE), np.float32),
        consts=consts,
    )
    shared = {k: np.ascontiguousarray(v) for k, v in shared.items()}

    in_maps = []
    zeros_pre = np.zeros((1024, 2048), np.float32)
    for c in range(8):
        b, hf = c // 2, c % 2
        m = dict(shared)
        m["x_fm"] = np.ascontiguousarray(x[b, hf * 2048:(hf + 1) * 2048, :].T)
        m["x_pre"] = (np.ascontiguousarray(x[b, 0:2048, :].T) if hf == 1
                      else zeros_pre)
        m["ropeC"] = ropeC1 if hf == 1 else ropeC0
        m["ropeS"] = ropeS1 if hf == 1 else ropeS0
        m["ropeCp"] = ropeC0
        m["ropeSp"] = ropeS0
        in_maps.append(m)
    return in_maps



ENGS = ("pe", "act", "dve", "sp", "gp", "gpc")
DMA_ENGS = ("sp", "gp")
RES_OF = {"gpc": "gp"}
RESRC = ("pe", "act", "dve", "sp", "gp")


class Sched:

    DEF_COST = {"pe": 400, "act": 600, "dve": 500, "sp": 1500, "gp": 1500,
                "gpc": 1300}

    def __init__(self):
        self.ops = []
        self.key_ops = {}
        self.cnt = {}
        self.epoch = 0
        self.final_raw = None

    def _resolve(self, d):
        if d is None:
            return None
        dk, dv = d
        lst = self.key_ops.get(dk)
        if not lst:
            return None
        step = 16 if dk not in ("pe", "act", "dve", "gpc") else 1
        idx = dv // step - 1
        if idx < 0:
            return None
        if idx >= len(lst):
            idx = len(lst) - 1
        return lst[idx]

    def do(self, e, fn, deps=(), key=None, cost=None):
        if e in DMA_ENGS:
            assert key is not None, "DMA op needs a semaphore key"
            step = 16
        else:
            key = e
            step = 1
            if e == "gpc":
                key = "gpc"
        gid = len(self.ops)
        dep_g = set()
        for d in deps:
            g = self._resolve(d)
            if g is not None:
                dep_g.add(g)
        self.cnt[key] = self.cnt.get(key, 0) + step
        self.key_ops.setdefault(key, []).append(gid)
        self.ops.append(dict(e=e, fn=fn, key=key, step=step, deps=dep_g,
                             cost=(cost if cost is not None
                                   else self.DEF_COST[e]), ep=self.epoch))
        return (key, self.cnt[key])

    def raw(self, e, fn, deps=()):
        dep_g = set()
        for d in deps:
            g = self._resolve(d)
            if g is not None:
                dep_g.add(g)
        self.final_raw = (e, fn, dep_g)

    def barrier(self):
        self.epoch += 1

    def finalize(self):
        import heapq
        ops = self.ops
        n = len(ops)
        res_of = lambda e: RES_OF.get(e, e)
        children = [[] for _ in range(n)]
        for i, o in enumerate(ops):
            for d in o["deps"]:
                children[d].append(i)
        prio = [0.0] * n
        for i in range(n - 1, -1, -1):
            m = 0.0
            for c in children[i]:
                if prio[c] > m:
                    m = prio[c]
            prio[i] = m + ops[i]["cost"]
        by_ep = {}
        for i, o in enumerate(ops):
            by_ep.setdefault(o["ep"], []).append(i)
        order = {e: [] for e in RESRC}
        t_base = 0.0
        for ep in sorted(by_ep):
            ids = by_ep[ep]
            npred = {}
            ready = {e: [] for e in RESRC}
            for i in ids:
                c = sum(1 for d in ops[i]["deps"] if ops[d]["ep"] == ep)
                npred[i] = c
                if c == 0:
                    heapq.heappush(ready[res_of(ops[i]["e"])], (-prio[i], i))
            eng_free = {e: t_base for e in RESRC}
            evq = []
            done = 0
            t_now = t_base
            t_end = t_base
            rdy_at = {}
            while done < len(ids):
                started = False
                for e in RESRC:
                    if eng_free[e] <= t_now and ready[e]:
                        _, i = heapq.heappop(ready[e])
                        st = max(t_now, rdy_at.get(i, t_now))
                        if st > t_now:
                            heapq.heappush(evq, (st, -1 - i))
                            continue
                        f = t_now + ops[i]["cost"]
                        eng_free[e] = f
                        heapq.heappush(evq, (f, i))
                        order[e].append(i)
                        started = True
                if started:
                    continue
                if not evq:
                    break
                t_now, i = heapq.heappop(evq)
                t_end = max(t_end, t_now)
                if i < 0:
                    i = -1 - i
                    heapq.heappush(ready[res_of(ops[i]["e"])], (-prio[i], i))
                    continue
                done += 1
                fe = ops[i]["e"]
                for c in children[i]:
                    if ops[c]["ep"] == ep:
                        lat = 100 if res_of(ops[c]["e"]) == res_of(fe) else 450
                        t_r = t_now + lat
                        if rdy_at.get(c, 0) < t_r:
                            rdy_at[c] = t_r
                        npred[c] -= 1
                        if npred[c] == 0:
                            heapq.heappush(ready[res_of(ops[c]["e"])],
                                           (-prio[c], c))
            t_base = t_end
        newval = [0] * n
        keypos = {}
        for e in RESRC:
            for i in order[e]:
                k = ops[i]["key"]
                keypos[k] = keypos.get(k, 0) + ops[i]["step"]
                newval[i] = keypos[k]
        ep_totals = {}
        run = {}
        eps = sorted(by_ep)
        pos_by_ep = {}
        for ep in eps:
            pos_by_ep[ep] = {}
        for e in RESRC:
            for i in order[e]:
                ep = ops[i]["ep"]
                k = ops[i]["key"]
                d = pos_by_ep[ep]
                d[k] = max(d.get(k, 0), newval[i])
        cum = {}
        for ep in eps:
            for k, v in pos_by_ep[ep].items():
                cum[k] = max(cum.get(k, 0), v)
            ep_totals[ep] = dict(cum)
        return order, newval, ep_totals


class Buf:
    __slots__ = ("w", "r")

    def __init__(self):
        self.w = None
        self.r = []

    def rdeps(self):
        return [self.w]

    def wdeps(self):
        return [self.w] + list(self.r)




def build_nc(debug_taps=()):
    nc = bass.Bass("TRN2", target_bir_lowering=False)

    D = {}
    for name, shape in [
        ("x_fm", (1024, 2048)), ("x_pre", (1024, 2048)),
        ("ropeC", (128, 2048)), ("ropeS", (128, 2048)),
        ("ropeCp", (128, 2048)), ("ropeSp", (128, 2048)),
        ("w1T", (1024, 256)), ("w2T", (256, 6144)),
        ("cw1T", (2048, 256)), ("cw2T", (256, 2048)),
        ("wexpT", (4096, 1024)), ("wconG", (4096, 1024)),
        ("scU", (17, 32, 128, 128)), ("scD", (17, 32, 128, 128)),
        ("bexp_sq", (128, 32)), ("ubias_col", (128, 32)),
        ("cbias_col", (128, 8)), ("dbias_col", (128, 8)),
        ("selSum", (8, 128, 48)), ("Ehead", (8, 128, 128)),
        ("E01", (128, 128)), ("E23", (128, 128)),
        ("ones_col", (128, 1)), ("ones_tile", (128, TILE)),
        ("consts", (128, 4)),
    ]:
        D[name] = nc.dram_tensor(name, list(shape), F32, kind="ExternalInput")
    out_fm = nc.dram_tensor("out_fm", [1024, 2048], F32, kind="ExternalOutput")
    x1_stash = nc.dram_tensor("x1_stash", [1024, 2048], F32, kind="Internal")
    m_stash = nc.dram_tensor("m_stash", [4096, 1024], BF16, kind="Internal")
    bperm = nc.dram_tensor("bperm", [32, 128], BF16, kind="Internal")
    zperm = nc.dram_tensor("zperm", [4096, 512], BF16, kind="Internal")
    w1e_stash = nc.dram_tensor("w1e_stash", [128, 32768], BF16, kind="Internal")

    S = Sched()

    def op(e, fn, reads=(), writes=(), deps=(), key=None, cost=None):
        d = list(deps)
        for b in reads:
            d += b.rdeps()
        for b in writes:
            d += b.wdeps()
        tok = S.do(e, fn, d, key=key, cost=cost)
        for b in reads:
            b.r.append(tok)
        for b in writes:
            b.w = tok
            b.r = []
        return tok

    taps = {}

    def tap(name, ap_, shape, dt, rbufs):
        if name not in debug_taps:
            return
        t = nc.dram_tensor("tap_" + name, list(shape), dt, kind="ExternalOutput")
        taps[name] = t
        op("sp", lambda sy, _a=ap_, _t=t: sy.dma_start(_t.ap(), _a),
           reads=list(rbufs), key="st_out_tap")

    with ExitStack() as top:

        def SBt(name, shape, dt, stack):
            h = stack.enter_context(nc.sbuf_tensor(name, list(shape), dt))
            return h, Buf()

        E01_sb, E01_b = SBt("E01_sb", [128, 128], BF16, top)
        E23_sb, E23_b = SBt("E23_sb", [128, 128], BF16, top)
        onesc_sb, onesc_b = SBt("onesc_sb", [128, 1], BF16, top)
        onest_sb, onest_b = SBt("onest_sb", [128, TILE], BF16, top)
        consts_sb, consts_b = SBt("consts_sb", [128, 4], F32, top)
        nc.const_aps.register(F32, EPS_RMS, consts_sb[:, 0:1])
        nc.const_aps.register(F32, EPS_LN, consts_sb[:, 1:2])
        nc.const_aps.register(F32, INV_C, consts_sb[:, 2:3])
        nc.const_aps.register(F32, INV_DH, consts_sb[:, 3:4])
        gbias, gbias_b = SBt("gbias", [128, 32], F32, top)
        obias, obias_b = SBt("obias", [128, 8], F32, top)

        op("gp", lambda g: g.dma_start(E01_sb[:], D["E01"].ap()), writes=[E01_b],
           key="res")
        op("gp", lambda g: g.dma_start(E23_sb[:], D["E23"].ap()), writes=[E23_b],
           key="res")
        op("gp", lambda g: g.dma_start(onesc_sb[:], D["ones_col"].ap()),
           writes=[onesc_b], key="res")
        op("gp", lambda g: g.dma_start(onest_sb[:], D["ones_tile"].ap()),
           writes=[onest_b], key="res")
        op("sp", lambda sy: sy.dma_start(consts_sb[:], D["consts"].ap()),
           writes=[consts_b], key="res_sp")

        def make_ln2(spadh, spad_b, srf, srf_b, srb, srb_b, vqh, vq_b, bankf,
                     gkey, skey):
            sf = lambda i: srf[0:1, i * TILE:(i + 1) * TILE]
            sb_ = lambda i: srb[0:1, i * TILE:(i + 1) * TILE]

            def ln(vch, v_bufs, outch, out_bufs, store_dst, t0, vall, oall):
                bsm, bsm_b = bankf()
                bsq, bsq_b = bankf()
                ch1, ch2 = [None], [None]
                for c in range(8):
                    op("dve", lambda v, _c=c: v.tensor_copy(vqh[:], vch(_c)),
                       reads=[v_bufs[c]], writes=[vq_b])
                    def m1(t, _f=(c == 0), _l=(c == 7)):
                        return t.matmul(bsm[0:1, :], onesc_sb[:], vqh[:],
                                        start=_f, stop=_l)
                    ch1[0] = op("pe", m1, reads=[vq_b, onesc_b],
                                writes=([bsm_b] if c == 7 else []),
                                deps=(bsm_b.wdeps() if c == 0 else [ch1[0]]))
                    op("act", lambda a, _c=c: a.activation(vqh[:], vch(_c), AF.Square),
                       reads=[v_bufs[c]], writes=[vq_b])
                    def m2(t, _f=(c == 0), _l=(c == 7)):
                        return t.matmul(bsq[0:1, :], onesc_sb[:], vqh[:],
                                        start=_f, stop=_l)
                    ch2[0] = op("pe", m2, reads=[vq_b, onesc_b],
                                writes=([bsq_b] if c == 7 else []),
                                deps=(bsq_b.wdeps() if c == 0 else [ch2[0]]))
                op("dve", lambda v: v.tensor_scalar(sf(2), bsm[0:1, :],
                                                    INV_C, None, ALU.mult),
                   reads=[bsm_b], writes=[srf_b])
                op("dve", lambda v: v.tensor_scalar(sf(0), bsq[0:1, :],
                                                    INV_C, None, ALU.mult),
                   reads=[bsq_b], writes=[srf_b])
                op("dve", lambda v: v.tensor_mul(sf(1), sf(2), sf(2)),
                   reads=[srf_b], writes=[srf_b])
                op("dve", lambda v: v.tensor_sub(sf(0), sf(0), sf(1)),
                   reads=[srf_b], writes=[srf_b])
                op("act", lambda a: a.activation(sf(0), sf(0), AF.Sqrt, bias=EPS_LN),
                   reads=[srf_b], writes=[srf_b], deps=[consts_b.w])
                op("dve", lambda v: v.reciprocal(sf(0), sf(0)),
                   reads=[srf_b], writes=[srf_b])
                op("dve", lambda v: v.tensor_mul(sf(3), sf(2), sf(0)),
                   reads=[srf_b], writes=[srf_b])
                op("act", lambda a: a.activation(sb_(0), sf(0), AF.Copy),
                   reads=[srf_b], writes=[srb_b])
                op("act", lambda a: a.activation(sb_(1), sf(3), AF.Copy),
                   reads=[srf_b], writes=[srb_b])
                op("dve", lambda v: v.tensor_sub(sf(4), sf(0), sb_(0)),
                   reads=[srf_b, srb_b], writes=[srf_b])
                op("dve", lambda v: v.tensor_sub(sf(5), sf(3), sb_(1)),
                   reads=[srf_b, srb_b], writes=[srf_b])
                op("gp", lambda g: g.dma_start(
                    AP(spadh, 0, [[64 * TILE, 2], [1, TILE]]),
                    AP(srb, 0, [[2 * TILE, 1], [TILE, 2], [1, TILE]])),
                   reads=[srb_b], writes=[spad_b], key=gkey)
                op("gp", lambda g: g.dma_start(
                    AP(spadh, 32 * TILE, [[64 * TILE, 2], [1, TILE]]),
                    AP(srf, 4 * TILE, [[8 * TILE, 1], [TILE, 2], [1, TILE]])),
                   reads=[srf_b], writes=[spad_b], key=gkey)
                brb, brb_b = bankf()
                op("pe", lambda t: t.matmul(brb[:], E01_sb[:], spadh[:],
                                            start=True, stop=True),
                   reads=[E01_b, spad_b], writes=[brb_b])
                bmb, bmb_b = bankf()
                op("pe", lambda t: t.matmul(bmb[:], E23_sb[:], spadh[:],
                                            start=True, stop=True),
                   reads=[E23_b, spad_b], writes=[bmb_b])
                brb_r = AP(brb, 0, [[TILE, 128], [0, 8], [1, TILE]])
                bmb_r = AP(bmb, 0, [[TILE, 128], [0, 8], [1, TILE]])
                op("dve", lambda v: v.tensor_tensor(oall, vall, brb_r, ALU.mult),
                   reads=list(v_bufs) + [brb_b], writes=list(out_bufs))
                op("dve", lambda v: v.tensor_tensor(oall, oall, bmb_r,
                                                    ALU.subtract),
                   reads=list(out_bufs) + [bmb_b], writes=list(out_bufs))
                o_ap = AP(store_dst, t0, [[2048, 128], [128 * 2048, 8],
                                          [1, TILE]])
                op("sp", lambda sy, _o=o_ap, _s=oall: sy.dma_start(_o, _s),
                   reads=list(out_bufs), key=skey)
            return ln

        def _c_off(c):
            return c * 128 * 2048

        qkv, _ = SBt("qkv", [128, 48 * TILE], BF16, top)
        qkv_bs = [Buf() for _ in range(48)]
        carry, carry_b = SBt("carry", [128, 16], F32, top)
        pacc, pacc_b = SBt("pacc", [128, 16], F32, top)
        w1e_b = Buf()

        qch = lambda i: qkv[:, i * TILE:(i + 1) * TILE]

        def chunk_load(dst_sb, src_d, nk, m, dst_b, key="res_w"):
            sap = AP(src_d, 0, [[m, 128], [128 * m, nk], [1, m]])
            dap = AP(dst_sb, 0, [[nk * m, 128], [m, nk], [1, m]])
            op("gp", lambda g, _d=dap, _s=sap: g.dma_start(_d, _s),
               writes=[dst_b], key=key)

        PUMP = [None]

        def _p(n=1):
            if PUMP[0] is None:
                return
            for _ in range(n):
                try:
                    next(PUMP[0])
                except StopIteration:
                    PUMP[0] = None
                    return

        def do_tile(E, ti, is_pre):
            (x_bf, x_bf_bs, x_f, x_f_bs, ropeC_sb, ropeC_b, ropeS_sb,
             ropeS_b, h2, h2_bs, h1s, h1s_bs, sqA, sqA_b, sqB, sqB_b,
             spadS, spadS_b, sbb, sbb_b, srp, srp_b, rqk, rqk_b,
             rqpad, rqpad_b, rqbg, rqbg_bs, scr1, scr1_b, scr2, scr2_b,
             scr3, scr3_b, cum, cum_bs, z1, z1_bs, uband, uband_b,
             sgband, sgband_b, vres, vres_bs, w1T_sb, w1T_b, w2T_sb,
             w2T_b, cw1T_sb, cw1T_b, cw2T_sb, cw2T_b, selSum_sb,
             selSum_b, Ehead_sb, Ehead_b, bankA, bqs, bqs_b, ln_A,
             rqb_g0, pscr1, pscr1_b, pscr2, pscr2_b, pscr3, pscr3_b,
             pscr4, pscr4_b) = E
            xch = lambda c: x_bf[:, c * TILE:(c + 1) * TILE]
            xbch = xch
            xfch = lambda c: x_f[:, c * TILE:(c + 1) * TILE]
            h2ch = lambda kk: h2[:, kk * TILE:(kk + 1) * TILE]
            z1ch = lambda kk: z1[:, kk * TILE:(kk + 1) * TILE]
            cumch = lambda i: cum[:, i * TILE:(i + 1) * TILE]
            vch = lambda c: vres[:, c * TILE:(c + 1) * TILE]
            rqb_w = (8 - rqb_g0) * TILE

            def rqb(gsel):
                return rqbg[:, (gsel - rqb_g0) * TILE:
                            (gsel - rqb_g0 + 1) * TILE]

            def w1T_t(k, m):
                return w1T_sb[:, k * 256 + m * 128: k * 256 + (m + 1) * 128]

            def w2T_t(kk, m):
                return w2T_sb[:, kk * 6144 + m * 128:
                              kk * 6144 + (m + 1) * 128]

            def cw1T_t(k, m):
                return cw1T_sb[:, k * 256 + m * 128: k * 256 + (m + 1) * 128]

            def cw2T_t(kk, m):
                return cw2T_sb[:, kk * 2048 + m * 128:
                               kk * 2048 + (m + 1) * 128]
            t0 = (ti % 4) * TILE
            xsrc = D["x_pre"] if is_pre else D["x_fm"]
            rCsrc = D["ropeCp"] if is_pre else D["ropeC"]
            rSsrc = D["ropeSp"] if is_pre else D["ropeS"]

            xin = AP(xsrc, t0, [[2048, 128], [128 * 2048, 8], [1, TILE]])
            xout = AP(x_bf, 0, [[8 * TILE, 128], [TILE, 8], [1, TILE]])
            op("gp", lambda g, _i=xin, _o=xout: g.dma_start(_o, _i),
               writes=x_bf_bs, key="x_bf", cost=4500)
            if not is_pre:
                xoutf = AP(x_f, 0, [[8 * TILE, 128], [TILE, 8], [1, TILE]])
                op("sp", lambda sy, _i=xin, _o=xoutf: sy.dma_start(_o, _i),
                   writes=x_f_bs, key="x_f", cost=4500)
            op("sp", lambda sy, _s=rCsrc, _t=t0: sy.dma_start(
                ropeC_sb[:], _s.ap()[:, _t:_t + TILE]), writes=[ropeC_b],
               key="ropeC")
            op("sp", lambda sy, _s=rSsrc, _t=t0: sy.dma_start(
                ropeS_sb[:], _s.ap()[:, _t:_t + TILE]), writes=[ropeS_b],
               key="ropeS")
            _p(2)

            bs_, bs_b = bankA()
            chf = [None]
            for c in range(8):
                st, st_b = (sqA, sqA_b) if c % 2 == 0 else (sqB, sqB_b)
                op("act", lambda a, _c=c, _s=st: a.activation(
                    _s[:], xch(_c), AF.Square),
                   reads=[x_bf_bs[c]], writes=[st_b])
                def fmm(t, _s=st, _f=(c == 0), _l=(c == 7), _b=bs_):
                    return t.matmul(_b[0:1, :], onesc_sb[:], _s[:],
                                    start=_f, stop=_l)
                chf[0] = op("pe", fmm, reads=[st_b, onesc_b],
                            writes=([bs_b] if c == 7 else []),
                            deps=(bs_b.wdeps() if c == 0 else [chf[0]]))
            _p(2)
            op("dve", lambda v, _b=bs_: v.tensor_copy(srp[0:1, :], _b[0:1, :]),
               reads=[bs_b], writes=[srp_b])
            op("act", lambda a: a.activation(srp[0:1, :], srp[0:1, :], AF.Sqrt,
                                             bias=EPS_RMS, scale=INV_C),
               reads=[srp_b], writes=[srp_b], deps=[consts_b.w])
            op("dve", lambda v: v.reciprocal(srp[0:1, :], srp[0:1, :]),
               reads=[srp_b], writes=[srp_b])
            op("gp", lambda g: g.dma_start(spadS[0:1, :], srp[0:1, :]),
               reads=[srp_b], writes=[spadS_b], key="spadS")
            bsb, bsb_b = bankA()
            op("pe", lambda t, _b=bsb: t.matmul(_b[:], E01_sb[:], spadS[:],
                                                start=True, stop=True),
               reads=[E01_b, spadS_b], writes=[bsb_b])
            op("dve", lambda v, _b=bsb: v.tensor_copy(sbb[:], _b[:]),
               reads=[bsb_b], writes=[sbb_b])
            _p(1)

            for m in range(2):
                bh, bh_b = bankA()
                def hmm(t, _m=m, _b=bh):
                    last = None
                    for k in range(8):
                        last = t.matmul(_b[:], w1T_t(k, _m), xch(k),
                                        start=(k == 0), stop=(k == 7))
                    return last
                op("pe", hmm, reads=[w1T_b] + x_bf_bs, writes=[bh_b],
                   cost=2600)
                op("dve", lambda v, _m=m, _b=bh: v.tensor_tensor(
                    h1s[:, _m * TILE:(_m + 1) * TILE], _b[:], sbb[:], ALU.mult),
                   reads=[bh_b, sbb_b], writes=[h1s_bs[m]])
                op("dve", lambda v, _m=m: v.scalar_tensor_tensor(
                    h2ch(_m), h1s[:, _m * TILE:(_m + 1) * TILE], 0.0,
                    h1s[:, _m * TILE:(_m + 1) * TILE], ALU.max, ALU.mult),
                   reads=[h1s_bs[m]], writes=[h2_bs[m]])
            _p(2)

            mlist = list(range(16, 48)) if is_pre else list(range(48))
            n_sums = 16 if is_pre else 32
            sum_cnt = [0]
            chs = [None]
            for m in mlist:
                bq, bq_b = bankA()
                def qmm(t, _m=m, _b=bq):
                    last = None
                    for kk in range(2):
                        last = t.matmul(_b[:], w2T_t(kk, _m), h2ch(kk),
                                        start=(kk == 0), stop=(kk == 1))
                    return last
                op("pe", qmm, reads=[w2T_b] + h2_bs, writes=[bq_b],
                   cost=750)
                if m % 2 == 0:
                    op("dve", lambda v, _m=m, _b=bq: v.tensor_copy(
                        qch(_m), _b[:]), reads=[bq_b], writes=[qkv_bs[m]])
                else:
                    op("act", lambda a, _m=m, _b=bq: a.activation(
                        qch(_m), _b[:], AF.Copy), reads=[bq_b],
                       writes=[qkv_bs[m]])
                if m % 8 == 7:
                    _p(2)
                if m < 32:
                    st, st_b = (sqA, sqA_b) if m % 2 == 0 else (sqB, sqB_b)
                    op("act", lambda a, _m=m, _s=st: a.activation(
                        _s[:], qch(_m), AF.Square),
                       reads=[qkv_bs[m]], writes=[st_b])
                    g = (m % 16) // 4
                    sel = g if m < 16 else 4 + g
                    sum_cnt[0] += 1
                    first = sum_cnt[0] == 1
                    last = sum_cnt[0] == n_sums
                    def smm(t, _sel=sel, _s=st, _f=first, _l=last):
                        return t.matmul(
                            bqs[0:48, :],
                            selSum_sb[:, _sel * 48:(_sel + 1) * 48],
                            _s[:], start=_f, stop=_l)
                    chs[0] = op("pe", smm, reads=[st_b, selSum_b],
                                writes=([bqs_b] if last else []),
                                deps=(bqs_b.wdeps() if first else [chs[0]]))

            op("dve", lambda v: v.tensor_copy(rqk[0:48, :], bqs[0:48, :]),
               reads=[bqs_b], writes=[rqk_b])
            op("act", lambda a: a.activation(rqk[0:48, :], rqk[0:48, :], AF.Sqrt,
                                             bias=EPS_RMS, scale=INV_DH),
               reads=[rqk_b], writes=[rqk_b])
            op("dve", lambda v: v.reciprocal(rqk[0:48, :], rqk[0:48, :]),
               reads=[rqk_b], writes=[rqk_b])
            op("dve", lambda v: v.tensor_copy(rqpad[0:48, :], rqk[0:48, :]),
               reads=[rqk_b], writes=[rqpad_b])
            _p(1)

            grange = range(4, 8) if is_pre else range(8)
            for gsel in grange:
                bb, bb_b = bankA()
                op("pe", lambda t, _g=gsel, _b=bb: t.matmul(
                    _b[:], Ehead_sb[:, _g * 128:(_g + 1) * 128], rqpad[:],
                    start=True, stop=True),
                   reads=[Ehead_b, rqpad_b], writes=[bb_b])
                op("dve", lambda v, _g=gsel, _b=bb: v.tensor_copy(
                    rqb(_g), _b[:]), reads=[bb_b], writes=[rqbg_bs[gsel]])
            for gsel in grange:
                base = (gsel - 4) * 4 + 16 if gsel >= 4 else gsel * 4
                qside = False
                ee = "gpc" if qside else "dve"
                s1, s1b = (pscr1, pscr1_b) if qside else (scr1, scr1_b)
                s2, s2b = (pscr2, pscr2_b) if qside else (scr2, scr2_b)
                s3, s3b = (pscr3, pscr3_b) if qside else (scr3, scr3_b)
                rqb_r = AP(rqbg, (gsel - rqb_g0) * TILE,
                           [[rqb_w, 128], [0, 4], [1, TILE]])
                op(ee, lambda v, _b=base, _r=rqb_r: v.tensor_tensor(
                    qkv[:, _b * TILE:(_b + 4) * TILE],
                    qkv[:, _b * TILE:(_b + 4) * TILE], _r, ALU.mult),
                   reads=[rqbg_bs[gsel]] + qkv_bs[base:base + 4],
                   writes=qkv_bs[base:base + 4],
                   cost=(4800 if qside else 1200))
                a_, b_ = base, base + 1
                rc_ = 1800 if qside else 500
                op(ee, lambda v, _a=a_, _s=s1: v.tensor_tensor(
                    _s[:], qch(_a), ropeC_sb[:], ALU.mult),
                   reads=[ropeC_b, qkv_bs[a_]], writes=[s1b], cost=rc_)
                op(ee, lambda v, _b=b_, _s=s2: v.tensor_tensor(
                    _s[:], qch(_b), ropeS_sb[:], ALU.mult),
                   reads=[ropeS_b, qkv_bs[b_]], writes=[s2b], cost=rc_)
                op(ee, lambda v, _a=a_, _s=s3: v.tensor_tensor(
                    _s[:], qch(_a), ropeS_sb[:], ALU.mult),
                   reads=[ropeS_b, qkv_bs[a_]], writes=[s3b], cost=rc_)
                op(ee, lambda v, _a=a_, _s1=s1, _s2=s2: v.tensor_add(
                    qch(_a), _s1[:], _s2[:]),
                   reads=[s1b, s2b], writes=[qkv_bs[a_]], cost=rc_)
                op(ee, lambda v, _b=b_: v.tensor_mul(
                    qch(_b), qch(_b), ropeC_sb[:]),
                   reads=[ropeC_b, qkv_bs[b_]], writes=[qkv_bs[b_]], cost=rc_)
                op(ee, lambda v, _b=b_, _s=s3: v.tensor_sub(
                    qch(_b), qch(_b), _s[:]),
                   reads=[s3b, qkv_bs[b_]], writes=[qkv_bs[b_]], cost=rc_)
                _p(1)

            for i in range(16):
                km, vm = 16 + i, 32 + i
                op("act", lambda a, _k=km: a.activation(
                    qch(_k), qch(_k), AF.Sigmoid),
                   reads=[qkv_bs[km]], writes=[qkv_bs[km]])
                if is_pre:
                    op("dve", lambda v, _k=km, _v=vm, _i=i:
                       v.scalar_tensor_tensor(
                           qch(_v), qch(_k), 1.0, qch(_v), ALU.mult,
                           ALU.mult, accum_out=pacc[:, _i:_i + 1]),
                       reads=[qkv_bs[km], qkv_bs[vm]],
                       writes=[qkv_bs[vm], pacc_b])
                else:
                    op("dve", lambda v, _k=km, _v=vm: v.tensor_mul(
                        qch(_v), qch(_k), qch(_v)),
                       reads=[qkv_bs[km], qkv_bs[vm]], writes=[qkv_bs[vm]])
                    if i % 4 == 3:
                        _p(1)
            if is_pre:
                op("dve", lambda v: v.tensor_add(carry[:], carry[:], pacc[:]),
                   reads=[pacc_b, carry_b], writes=[carry_b])
                _p(2)
                return

            for i in range(16):
                vm = 32 + i
                op("dve", lambda v, _i=i, _v=vm: v.tensor_tensor_scan(
                    cumch(_i), onest_sb[:], qch(_v), carry[:, _i:_i + 1],
                    ALU.mult, ALU.add),
                   reads=[onest_b, qkv_bs[vm], carry_b], writes=[cum_bs[i]],
                   cost=1200)
                op("act", lambda a, _i=i: a.activation(
                    qch(_i), qch(_i), AF.Relu),
                   reads=[qkv_bs[i]], writes=[qkv_bs[i]])
                op("dve", lambda v, _i=i: v.tensor_mul(
                    qch(_i), qch(_i), cumch(_i)),
                   reads=[qkv_bs[i], cum_bs[i]], writes=[qkv_bs[i]])
            op("dve", lambda v: v.tensor_copy(
                carry[:], AP(cum, TILE - 1, [[16 * TILE, 128], [TILE, 16]])),
               reads=cum_bs, writes=[carry_b])

            tap("y_t%d" % ti, qkv[:, 0:16 * TILE], (128, 16 * TILE), BF16,
                qkv_bs[0:16])

            for m in range(2):
                bz, bz_b = bankA()
                def zmm(t, _m=m, _b=bz):
                    last = None
                    for k in range(16):
                        last = t.matmul(_b[:], cw1T_t(k, _m), qch(k),
                                        start=(k == 0), stop=(k == 15))
                    return last
                op("pe", zmm, reads=[cw1T_b] + qkv_bs[0:16],
                   writes=[bz_b], cost=5200)
                op("dve", lambda v, _m=m, _b=bz: v.tensor_copy(
                    z1ch(_m), _b[:]), reads=[bz_b], writes=[z1_bs[m]])

            for j in range(8):
                bu, bu_b = bankA()
                def umm(t, _j=j, _b=bu):
                    last = None
                    for kk in range(2):
                        last = t.matmul(_b[:], cw2T_t(kk, _j), z1ch(kk),
                                        start=(kk == 0), stop=(kk == 1))
                    return last
                op("pe", umm, reads=[cw2T_b] + z1_bs, writes=[bu_b],
                   cost=750)
                bg, bg_b = bankA()
                def gmm(t, _j=j, _b=bg):
                    last = None
                    for kk in range(2):
                        last = t.matmul(_b[:], cw2T_t(kk, 8 + _j), z1ch(kk),
                                        start=(kk == 0), stop=(kk == 1))
                    return last
                op("pe", gmm, reads=[cw2T_b] + z1_bs, writes=[bg_b],
                   cost=750)
                op("dve", lambda v, _b=bu: v.tensor_copy(uband[:], _b[:]),
                   reads=[bu_b], writes=[uband_b])
                op("act", lambda a, _b=bg: a.activation(
                    sgband[:], _b[:], AF.Silu), reads=[bg_b], writes=[sgband_b])
                op("dve", lambda v: v.tensor_mul(uband[:], uband[:], sgband[:]),
                   reads=[uband_b, sgband_b], writes=[uband_b])
                op("dve", lambda v, _j=j: v.scalar_tensor_tensor(
                    vch(_j), xfch(_j), 2.0, uband[:], ALU.mult, ALU.add),
                   reads=[uband_b, x_f_bs[j]], writes=[vres_bs[j]])

            ln_A(vch, vres_bs, xfch, x_f_bs, x1_stash, t0,
                 vres[:, 0:8 * TILE], x_f[:, 0:8 * TILE])
            tap("x1_t%d" % ti, x_f[:], (128, 8 * TILE), F32, x_f_bs)

        with ExitStack() as ph1:
            x_bf, _ = SBt("x_bf", [128, 8 * TILE], BF16, ph1)
            x_bf_bs = [Buf() for _ in range(8)]
            ropeC_sb, ropeC_b = SBt("ropeC_sb", [128, TILE], F32, ph1)
            ropeS_sb, ropeS_b = SBt("ropeS_sb", [128, TILE], F32, ph1)
            h2, _ = SBt("h2", [128, 2 * TILE], BF16, ph1)
            h2_bs = [Buf() for _ in range(2)]
            h1s, _ = SBt("h1s", [128, 2 * TILE], BF16, ph1)
            h1s_bs = [Buf() for _ in range(2)]
            sqA, sqA_b = SBt("sqA", [128, TILE], BF16, ph1)
            sqB, sqB_b = SBt("sqB", [128, TILE], BF16, ph1)
            spadS, spadS_b = SBt("spadS", [128, TILE], BF16, ph1)
            sbb, sbb_b = SBt("sbb", [128, TILE], F32, ph1)
            srp, srp_b = SBt("srp", [1, TILE], F32, ph1)
            rqk, rqk_b = SBt("rqk", [64, TILE], F32, ph1)
            rqpad, rqpad_b = SBt("rqpad", [128, TILE], BF16, ph1)
            rqbg, _ = SBt("rqbg", [128, 4 * TILE], BF16, ph1)
            rqbg_bs = [Buf() for _ in range(8)]
            scr1, scr1_b = SBt("scr1", [128, TILE], BF16, ph1)
            scr2, scr2_b = SBt("scr2", [128, TILE], BF16, ph1)
            scr3, scr3_b = SBt("scr3", [128, TILE], BF16, ph1)
            w1T_sb, w1T_b = SBt("w1T_sb", [128, 8 * 256], BF16, ph1)
            w2T_sb, w2T_b = SBt("w2T_sb", [128, 2 * 6144], BF16, ph1)
            selSum_sb, selSum_b = SBt("selSum_sb", [128, 8 * 48], BF16, ph1)
            Ehead_sb, Ehead_b = SBt("Ehead_sb", [128, 8 * 128], BF16, ph1)
            x_f = x_f_bs = cum = cum_bs = vres = vres_bs = None
            z1 = z1_bs = uband = uband_b = sgband = sgband_b = None
            cw1T_sb = cw1T_b = cw2T_sb = cw2T_b = ln_A = None

            chunk_load(w1T_sb, D["w1T"], 8, 256, w1T_b)
            chunk_load(w2T_sb, D["w2T"], 2, 6144, w2T_b)
            chunk_load(selSum_sb, D["selSum"], 8, 48, selSum_b)
            chunk_load(Ehead_sb, D["Ehead"], 8, 128, Ehead_b)

            psP = [(ph1.enter_context(
                nc.psum_tensor("psP%d" % i, [128, TILE], F32)), Buf())
                for i in range(8)]
            pa_i = [0]

            def bankA():
                h, b = psP[pa_i[0] % 3]
                pa_i[0] += 1
                return h, b

            pw_i = [0]

            def bankB():
                h, b = psP[3 + pw_i[0] % 4]
                pw_i[0] += 1
                return h, b

            bqs, bqs_b = psP[7]

            op("dve", lambda v, _r=rqpad: v.memset(_r[:], 0.0),
               writes=[rqpad_b])
            op("dve", lambda v, _s=spadS: v.memset(_s[:], 0.0),
               writes=[spadS_b])
            op("dve", lambda v: v.memset(carry[:], 0.0), writes=[carry_b])

            sL, sL_b = SBt("sL", [128, 16 * 128], BF16, ph1)
            sM, sM_b = SBt("sM", [128, 16 * 128], BF16, ph1)

            def pbuf(name):
                h = ph1.enter_context(nc.sbuf_tensor(name, [128, 32 * 128],
                                                     BF16))
                return h, [Buf() for _ in range(32)]

            Xh, X_bs = pbuf("Xh")
            Yh, Y_bs = pbuf("Yh")
            BlT, BlT_bs = pbuf("BlT")
            BhT, BhT_bs = pbuf("BhT")
            Dh, Dh_bs = pbuf("Dh")
            Dl, Dl_bs = pbuf("Dl")
            Zh, _ = SBt("Zh", [128, 32 * 512], BF16, ph1)
            Zh_bs = [Buf() for _ in range(32)]
            wring = [SBt("wring%d" % i, [128, 512], BF16, ph1)
                     for i in range(3)]
            wstage = [SBt("wstage%d" % i, [128, 512], BF16, ph1)
                      for i in range(2)]
            bcolZ, bcolZ_b = SBt("bcolZ", [128, 32], BF16, ph1)
            bcolG, bcolG_b = SBt("bcolG", [128, 32], BF16, ph1)
            ub_sb, ub_sb_b = SBt("ub_sb", [128, 32], F32, ph1)
            cb_sb, cb_sb_b = SBt("cb_sb", [128, 8], F32, ph1)
            db_sb, db_sb_b = SBt("db_sb", [128, 8], F32, ph1)
            bexp_sb, bexp_sb_b = SBt("bexp_sb", [128, 32], BF16, ph1)

            Zp = 32 * 512

            def build_emit():
                op("sp", lambda sy: sy.dma_start(ub_sb[:], D["ubias_col"].ap()),
                   writes=[ub_sb_b], key="res_b")
                op("sp", lambda sy: sy.dma_start(cb_sb[:], D["cbias_col"].ap()),
                   writes=[cb_sb_b], key="res_b")
                op("sp", lambda sy: sy.dma_start(db_sb[:], D["dbias_col"].ap()),
                   writes=[db_sb_b], key="res_b")
                op("gp", lambda g: g.dma_start(bexp_sb[:], D["bexp_sq"].ap()),
                   writes=[bexp_sb_b], key="res_bg")
                op("dve", lambda v: v.tensor_add(obias[:], cb_sb[:], db_sb[:]),
                   reads=[cb_sb_b, db_sb_b], writes=[obias_b])
                yield

                def half_load(eng, buf, buf_b, src, slot, h, key):
                    off = slot * 524288 + h * 262144
                    op(eng, lambda g, _b=buf, _o=off, _s=src: g.dma_start(
                        AP(_b, 0, [[2048, 128], [128, 16], [1, 128]]),
                        AP(_s, _o, [[128, 128], [16384, 16], [1, 128]])),
                       writes=[buf_b], key=key, cost=2300)

                def tree_mul(Lspec, Mspec, dst, dst_bs):
                    for h in range(2):
                        if Lspec[0] == "h":
                            half_load("gp", sL, sL_b, Lspec[1], Lspec[2], h,
                                      "sL")
                        if Mspec[0] == "h":
                            half_load("gp", sM, sM_b, Mspec[1], Mspec[2], h,
                                      "sM")
                        for cl in range(16):
                            cc = h * 16 + cl
                            if Lspec[0] == "h":
                                lap = sL[:, cl * 128:(cl + 1) * 128]
                                lrd = [sL_b]
                            else:
                                lap = Lspec[1][:, cc * 128:(cc + 1) * 128]
                                lrd = [Lspec[2][cc]]
                            if Mspec[0] == "h":
                                map_ = sM[:, cl * 128:(cl + 1) * 128]
                                mrd = [sM_b]
                            else:
                                map_ = Mspec[1][:, cc * 128:(cc + 1) * 128]
                                mrd = [Mspec[2][cc]]
                            bp, bp_b = bankB()
                            op("pe", lambda t, _l=lap, _m=map_, _b=bp:
                               t.matmul(_b[:, 0:128], _l, _m,
                                        start=True, stop=True),
                               reads=lrd + mrd, writes=[bp_b], cost=350)
                            ee = "dve" if cc % 2 == 0 else "act"
                            op(ee, lambda v, _b=bp, _cc=cc, _d=dst:
                               (v.tensor_copy(
                                   _d[:, _cc * 128:(_cc + 1) * 128],
                                   _b[:, 0:128]) if _cc % 2 == 0 else
                                v.activation(
                                   _d[:, _cc * 128:(_cc + 1) * 128],
                                   _b[:, 0:128], AF.Copy)),
                               reads=[bp_b], writes=[dst_bs[cc]], cost=400)
                        yield

                H = lambda src, slot: ("h", src, slot)
                R = lambda buf, bufs: ("r", buf, bufs)
                scU_, scD_ = D["scU"], D["scD"]
                yield from tree_mul(H(scU_, 12), H(scU_, 0), BlT, BlT_bs)
                yield from tree_mul(H(scU_, 2), H(scU_, 13), Xh, X_bs)
                yield from tree_mul(H(scU_, 14), H(scU_, 4), Yh, Y_bs)
                yield from tree_mul(R(Xh, X_bs), R(BlT, BlT_bs), BlT, BlT_bs)
                yield from tree_mul(R(Yh, Y_bs), H(scU_, 6), Yh, Y_bs)
                yield from tree_mul(R(BlT, BlT_bs), R(Yh, Y_bs), BlT, BlT_bs)
                yield from tree_mul(H(scU_, 15), H(scU_, 7), BhT, BhT_bs)
                yield from tree_mul(H(scU_, 9), H(scU_, 16), Xh, X_bs)
                yield from tree_mul(R(Xh, X_bs), R(BhT, BhT_bs), BhT, BhT_bs)
                yield from tree_mul(R(BhT, BhT_bs), H(scU_, 11), BhT, BhT_bs)
                yield from tree_mul(H(scD_, 16), H(scD_, 11), Dh, Dh_bs)
                yield from tree_mul(H(scD_, 9), H(scD_, 15), Xh, X_bs)
                yield from tree_mul(R(Xh, X_bs), R(Dh, Dh_bs), Dh, Dh_bs)
                yield from tree_mul(R(Dh, Dh_bs), H(scD_, 7), Dh, Dh_bs)
                yield from tree_mul(H(scD_, 14), H(scD_, 6), Dl, Dl_bs)
                yield from tree_mul(H(scD_, 4), H(scD_, 13), Xh, X_bs)
                yield from tree_mul(H(scD_, 12), H(scD_, 2), Yh, Y_bs)
                yield from tree_mul(R(Xh, X_bs), R(Dl, Dl_bs), Dl, Dl_bs)
                yield from tree_mul(R(Yh, Y_bs), H(scD_, 0), Yh, Y_bs)
                yield from tree_mul(R(Dl, Dl_bs), R(Yh, Y_bs), Dl, Dl_bs)

                zp_b = Buf()

                def bounce_fwd():
                    for b4 in range(4):
                        op("sp", lambda sy, _b=b4: sy.dma_start(
                            AP(zperm, 32 * _b * 512,
                               [[65536, 32], [512, 32], [1, 512]]),
                            AP(Zh, _b * Zp,
                               [[4 * Zp, 32], [512, 32], [1, 512]])),
                           reads=Zh_bs, writes=([zp_b] if b4 == 3 else []),
                           deps=(zp_b.wdeps() if b4 == 0 else ()),
                           key="st_zp")
                    op("gp", lambda g: g.dma_start(
                        AP(Zh, 0, [[Zp, 128], [512, 32], [1, 512]]),
                        AP(zperm, 0, [[512, 128], [65536, 32], [1, 512]])),
                       reads=[zp_b], writes=Zh_bs, key="perm")
                    yield

                def bounce_inv():
                    op("sp", lambda sy: sy.dma_start(
                        AP(zperm, 0, [[512, 128], [65536, 32], [1, 512]]),
                        AP(Zh, 0, [[Zp, 128], [512, 32], [1, 512]])),
                       reads=Zh_bs, writes=[zp_b], key="st_zp")
                    op("gp", lambda g: g.dma_start(
                        AP(Zh, 0, [[Zp, 128], [512, 32], [1, 512]]),
                        AP(zperm, 0, [[16384, 128], [512, 32], [1, 512]])),
                       reads=[zp_b], writes=Zh_bs, key="perm")
                    yield

                for half in range(2):
                    for cch in range(32):
                        wr, wr_b = wring[cch % 3]
                        woff = cch * 128 * 1024 + half * 512
                        op("gp", lambda g, _w=wr, _o=woff: g.dma_start(
                            _w[:], AP(D["wexpT"], _o, [[1024, 128], [1, 512]])),
                           writes=[wr_b], key="wring%d" % (cch % 3))
                        bz, bz_b = bankB()
                        op("pe", lambda t, _c=cch, _w=wr, _b=bz: t.matmul(
                            _b[:], BlT[:, _c * 128:(_c + 1) * 128], _w[:],
                            start=True, stop=True),
                           reads=[BlT_bs[cch], wr_b], writes=[bz_b])
                        op("dve", lambda v, _c=cch, _b=bz: v.tensor_copy(
                            Zh[:, _c * 512:(_c + 1) * 512], _b[:]),
                           reads=[bz_b], writes=[Zh_bs[cch]])
                        if cch % 8 == 7:
                            yield
                    yield from bounce_fwd()
                    for gc in range(32):
                        ws, ws_b = wstage[gc % 2]
                        for dc in range(4):
                            bf_, bf_b = bankB()
                            op("pe", lambda t, _g=gc, _d=dc, _b=bf_: t.matmul(
                                _b[:, 0:128],
                                Zh[:, _g * 512 + _d * 128:
                                   _g * 512 + (_d + 1) * 128],
                                BhT[:, _g * 128:(_g + 1) * 128],
                                start=True, stop=True),
                               reads=[Zh_bs[gc], BhT_bs[gc]],
                               writes=[bf_b])
                            if dc % 2 == 0:
                                op("dve", lambda v, _b=bf_, _d=dc, _w=ws:
                                   v.tensor_copy(
                                       _w[:, _d * 128:(_d + 1) * 128],
                                       _b[:, 0:128]),
                                   reads=[bf_b],
                                   deps=(ws_b.wdeps() if dc == 0 else ()))
                            else:
                                op("act", lambda a, _b=bf_, _d=dc, _w=ws:
                                   a.activation(
                                       _w[:, _d * 128:(_d + 1) * 128],
                                       _b[:, 0:128], AF.Copy),
                                   reads=[bf_b],
                                   writes=([ws_b] if dc == 3 else []))
                        wap = AP(w1e_stash, half * 16384 + gc * 128,
                                 [[32768, 128], [4096, 4], [1, 128]])
                        op("sp", lambda sy, _a=wap, _w=ws: sy.dma_start(
                            _a, _w[:]), reads=[ws_b], writes=[w1e_b],
                           key="st_w1e")
                        if gc % 8 == 7:
                            yield

                bzb, bzb_b = bankB()
                chb = [None]
                for cch in range(32):
                    def bmm(t, _c=cch, _b=bzb):
                        return t.matmul(_b[:, _c:_c + 1],
                                        BlT[:, _c * 128:(_c + 1) * 128],
                                        bexp_sb[:, _c:_c + 1],
                                        start=True, stop=True)
                    chb[0] = op("pe", bmm,
                                reads=[BlT_bs[cch], bexp_sb_b],
                                writes=([bzb_b] if cch == 31 else []),
                                deps=(bzb_b.wdeps() if cch == 0 else [chb[0]]))
                op("dve", lambda v: v.tensor_copy(bcolZ[:], bzb[:, 0:32]),
                   reads=[bzb_b], writes=[bcolZ_b])
                tok_bp = op("sp", lambda sy: sy.dma_start(
                    AP(bperm, 0, [[32, 128], [1, 32]]), bcolZ[:]),
                    reads=[bcolZ_b], key="st_bp")
                op("sp", lambda sy: sy.dma_start_transpose(
                    bcolG[:], bperm.ap()),
                   writes=[bcolG_b], deps=[tok_bp], key="bcolG")
                yield
                bgb, bgb_b = bankB()
                chg = [None]
                for gc in range(32):
                    def gmm2(t, _g=gc, _b=bgb):
                        return t.matmul(_b[:, _g:_g + 1],
                                        BhT[:, _g * 128:(_g + 1) * 128],
                                        bcolG[:, _g:_g + 1],
                                        start=True, stop=True)
                    chg[0] = op("pe", gmm2,
                                reads=[BhT_bs[gc], bcolG_b],
                                writes=([bgb_b] if gc == 31 else []),
                                deps=(bgb_b.wdeps() if gc == 0 else [chg[0]]))
                op("dve", lambda v: v.tensor_copy(gbias[:], bgb[:, 0:32]),
                   reads=[bgb_b], writes=[gbias_b])
                op("dve", lambda v: v.tensor_add(gbias[:], gbias[:], ub_sb[:]),
                   reads=[gbias_b, ub_sb_b], writes=[gbias_b])
                yield

                for half in range(2):
                    for gc in range(32):
                        wr, wr_b = wring[gc % 3]
                        woff = gc * 128 * 1024 + half * 512
                        op("gp", lambda g, _w=wr, _o=woff: g.dma_start(
                            _w[:], AP(D["wconG"], _o,
                                      [[1024, 128], [1, 512]])),
                           writes=[wr_b], key="wring%d" % (gc % 3))
                        by, by_b = bankB()
                        op("pe", lambda t, _g=gc, _w=wr, _b=by: t.matmul(
                            _b[:], Dh[:, _g * 128:(_g + 1) * 128], _w[:],
                            start=True, stop=True),
                           reads=[Dh_bs[gc], wr_b], writes=[by_b])
                        op("dve", lambda v, _g=gc, _b=by: v.tensor_copy(
                            Zh[:, _g * 512:(_g + 1) * 512], _b[:]),
                           reads=[by_b], writes=[Zh_bs[gc]])
                        if gc % 8 == 7:
                            yield
                    yield from bounce_inv()
                    for cch in range(32):
                        bm, bm_b = bankB()
                        op("pe", lambda t, _c=cch, _b=bm: t.matmul(
                            _b[:], Dl[:, _c * 128:(_c + 1) * 128],
                            Zh[:, _c * 512:(_c + 1) * 512],
                            start=True, stop=True),
                           reads=[Dl_bs[cch], Zh_bs[cch]],
                           writes=[bm_b])
                        op("dve", lambda v, _c=cch, _b=bm: v.tensor_copy(
                            Zh[:, _c * 512:(_c + 1) * 512], _b[:]),
                           reads=[bm_b], writes=[Zh_bs[cch]])
                        dap = AP(m_stash, cch * 128 * 1024 + half * 512,
                                 [[1024, 128], [1, 512]])
                        op("sp", lambda sy, _d=dap, _c=cch: sy.dma_start(
                            _d, Zh[:, _c * 512:(_c + 1) * 512]),
                           reads=[Zh_bs[cch]], key="st_m")
                        if cch % 8 == 7:
                            yield

            E1 = (x_bf, x_bf_bs, None, None, ropeC_sb, ropeC_b, ropeS_sb,
                  ropeS_b, h2, h2_bs, h1s, h1s_bs, sqA, sqA_b, sqB, sqB_b,
                  spadS, spadS_b, sbb, sbb_b, srp, srp_b, rqk, rqk_b,
                  rqpad, rqpad_b, rqbg, rqbg_bs, scr1, scr1_b, scr2, scr2_b,
                  scr3, scr3_b, None, None, None, None, None, None,
                  None, None, None, None, w1T_sb, w1T_b, w2T_sb,
                  w2T_b, None, None, None, None, selSum_sb,
                  selSum_b, Ehead_sb, Ehead_b, bankA, bqs, bqs_b, None,
                  4, None, None, None, None, None, None, None, None)
            PUMP[0] = build_emit()
            _p(4)
            for ti in range(NPT):
                do_tile(E1, ti, True)
            while PUMP[0] is not None:
                _p(1)
            tap("w1eff", AP(w1e_stash, 0, [[32768, 128], [1, 32768]]),
                (128, 32768), BF16, [w1e_b])
            tap("blt", BlT[:], (128, 4096), BF16, BlT_bs)
            tap("bht", BhT[:], (128, 4096), BF16, BhT_bs)
            tap("dh", Dh[:], (128, 4096), BF16, Dh_bs)
            tap("dl", Dl[:], (128, 4096), BF16, Dl_bs)
            tap("zh", Zh[:], (128, 16384), BF16, Zh_bs)
            tap("mstash", AP(m_stash, 0, [[1024, 4096], [1, 1024]]),
                (4096, 1024), BF16, [])

        S.barrier()
        with ExitStack() as ph2:
            x_bf, _ = SBt("x_bf2", [128, 8 * TILE], BF16, ph2)
            x_bf_bs = [Buf() for _ in range(8)]
            x_f, _ = SBt("x_f", [128, 8 * TILE], F32, ph2)
            x_f_bs = [Buf() for _ in range(8)]
            ropeC_sb, ropeC_b = SBt("ropeC2_sb", [128, TILE], F32, ph2)
            ropeS_sb, ropeS_b = SBt("ropeS2_sb", [128, TILE], F32, ph2)
            h2, _ = SBt("h2b", [128, 2 * TILE], BF16, ph2)
            h2_bs = [Buf() for _ in range(2)]
            h1s, _ = SBt("h1sb", [128, 2 * TILE], BF16, ph2)
            h1s_bs = [Buf() for _ in range(2)]
            sqA, sqA_b = SBt("sqA2", [128, TILE], BF16, ph2)
            sqB, sqB_b = SBt("sqB2", [128, TILE], BF16, ph2)
            spadS, spadS_b = SBt("spadS2", [128, TILE], BF16, ph2)
            spadL, spadL_b = SBt("spadL", [128, TILE], BF16, ph2)
            sbb, sbb_b = SBt("sbb2", [128, TILE], F32, ph2)
            srp, srp_b = SBt("srp2", [1, TILE], F32, ph2)
            srowF, srowF_b = SBt("srowF", [1, 6 * TILE], F32, ph2)
            srowB, srowB_b = SBt("srowB", [1, 2 * TILE], BF16, ph2)
            rqk, rqk_b = SBt("rqk2", [64, TILE], F32, ph2)
            rqpad, rqpad_b = SBt("rqpad2", [128, TILE], BF16, ph2)
            rqbg, _ = SBt("rqbg2", [128, 8 * TILE], BF16, ph2)
            rqbg_bs = [Buf() for _ in range(8)]
            scr1, scr1_b = SBt("scr1b", [128, TILE], BF16, ph2)
            scr2, scr2_b = SBt("scr2b", [128, TILE], BF16, ph2)
            scr3, scr3_b = SBt("scr3b", [128, TILE], BF16, ph2)
            pscr1, pscr1_b = SBt("pscr1", [128, TILE], BF16, ph2)
            pscr2, pscr2_b = SBt("pscr2", [128, TILE], BF16, ph2)
            pscr3, pscr3_b = SBt("pscr3", [128, TILE], BF16, ph2)
            pscr4, pscr4_b = SBt("pscr4", [128, TILE], F32, ph2)
            cum, _ = SBt("cum", [128, 16 * TILE], BF16, ph2)
            cum_bs = [Buf() for _ in range(16)]
            z1, _ = SBt("z1", [128, 2 * TILE], BF16, ph2)
            z1_bs = [Buf() for _ in range(2)]
            uband, uband_b = SBt("uband", [128, TILE], BF16, ph2)
            sgband, sgband_b = SBt("sgband", [128, TILE], BF16, ph2)
            vres, _ = SBt("vres", [128, 8 * TILE], F32, ph2)
            vres_bs = [Buf() for _ in range(8)]
            vqA, vqA_b = SBt("vqA", [128, TILE], BF16, ph2)
            w1T_sb, w1T_b = SBt("w1T2_sb", [128, 8 * 256], BF16, ph2)
            w2T_sb, w2T_b = SBt("w2T2_sb", [128, 2 * 6144], BF16, ph2)
            cw1T_sb, cw1T_b = SBt("cw1T_sb", [128, 16 * 256], BF16, ph2)
            cw2T_sb, cw2T_b = SBt("cw2T_sb", [128, 2 * 2048], BF16, ph2)
            selSum_sb, selSum_b = SBt("selSum2_sb", [128, 8 * 48], BF16, ph2)
            Ehead_sb, Ehead_b = SBt("Ehead2_sb", [128, 8 * 128], BF16, ph2)

            chunk_load(w1T_sb, D["w1T"], 8, 256, w1T_b, key="res_w2")
            chunk_load(w2T_sb, D["w2T"], 2, 6144, w2T_b, key="res_w2")
            chunk_load(cw1T_sb, D["cw1T"], 16, 256, cw1T_b, key="res_w2")
            chunk_load(cw2T_sb, D["cw2T"], 2, 2048, cw2T_b, key="res_w2")
            chunk_load(selSum_sb, D["selSum"], 8, 48, selSum_b, key="res_w2")
            chunk_load(Ehead_sb, D["Ehead"], 8, 128, Ehead_b, key="res_w2")

            psA = [(ph2.enter_context(
                nc.psum_tensor("psA%d" % i, [128, TILE], F32)), Buf())
                for i in range(8)]
            bank_i = [0]

            def bankA():
                h, b = psA[bank_i[0] % 7]
                bank_i[0] += 1
                return h, b

            bqs, bqs_b = psA[7]

            op("dve", lambda v, _r=rqpad: v.memset(_r[:], 0.0),
               writes=[rqpad_b])
            op("dve", lambda v, _s=spadS: v.memset(_s[:], 0.0),
               writes=[spadS_b])
            op("dve", lambda v, _s=spadL: v.memset(_s[:], 0.0),
               writes=[spadL_b])

            ln_A = make_ln2(spadL, spadL_b, srowF, srowF_b, srowB, srowB_b,
                            vqA, vqA_b, bankA, "spadL", "st_x1")

            E2 = (x_bf, x_bf_bs, x_f, x_f_bs, ropeC_sb, ropeC_b, ropeS_sb,
                  ropeS_b, h2, h2_bs, h1s, h1s_bs, sqA, sqA_b, sqB, sqB_b,
                  spadS, spadS_b, sbb, sbb_b, srp, srp_b, rqk, rqk_b,
                  rqpad, rqpad_b, rqbg, rqbg_bs, scr1, scr1_b, scr2, scr2_b,
                  scr3, scr3_b, cum, cum_bs, z1, z1_bs, uband, uband_b,
                  sgband, sgband_b, vres, vres_bs, w1T_sb, w1T_b, w2T_sb,
                  w2T_b, cw1T_sb, cw1T_b, cw2T_sb, cw2T_b, selSum_sb,
                  selSum_b, Ehead_sb, Ehead_b, bankA, bqs, bqs_b, ln_A,
                  0, pscr1, pscr1_b, pscr2, pscr2_b, pscr3, pscr3_b,
                  pscr4, pscr4_b)
            for ti in range(NT):
                do_tile(E2, ti, False)

        S.barrier()
        with ExitStack() as phc:
            x1_bf, _ = SBt("x1_bf", [128, 8 * TILE], BF16, phc)
            x1_bf_bs = [Buf() for _ in range(8)]
            x1_f = [SBt("x1_f%d" % p, [128, 8 * TILE], F32, phc)[0] for p in range(2)]
            x1_f_bs = [[Buf() for _ in range(8)] for p in range(2)]
            ug = [SBt("ug%d" % p, [128, 32 * TILE], BF16, phc)[0] for p in range(2)]
            ug_bs = [[Buf() for _ in range(32)] for p in range(2)]
            mring = [SBt("mring%d" % i, [128, 1024], BF16, phc) for i in range(3)]
            w1ring = [SBt("w1ring%d" % i, [128, 1024], BF16, phc)
                      for i in range(4)]
            spadL2, spadL2_b = SBt("spadL2", [128, TILE], BF16, phc)
            srowF2, srowF2_b = SBt("srowF2", [1, 8 * TILE], F32, phc)
            srowB2, srowB2_b = SBt("srowB2", [1, 2 * TILE], BF16, phc)
            vq2, vq2_b = SBt("vq2", [128, TILE], BF16, phc)

            psC = [(phc.enter_context(nc.psum_tensor("psC%d" % i, [128, TILE], F32)),
                    Buf()) for i in range(8)]
            bc_i = [0]

            def bankC():
                h, b = psC[bc_i[0] % 4]
                bc_i[0] += 1
                return h, b

            op("dve", lambda v: v.memset(spadL2[:], 0.0), writes=[spadL2_b])
            ln_B = make_ln2(spadL2, spadL2_b, srowF2, srowF2_b, srowB2, srowB2_b,
                            vq2, vq2_b, bankC, "spadL2", "st_out")

            x1ch = lambda c: x1_bf[:, c * TILE:(c + 1) * TILE]



            ob_store = {}

            def stage_L(ti):
                p = ti % 2
                t0 = ti * TILE
                xin = AP(x1_stash, t0, [[2048, 128], [128 * 2048, 8], [1, TILE]])
                xo_bf = AP(x1_bf, 0, [[8 * TILE, 128], [TILE, 8], [1, TILE]])
                xo_f = AP(x1_f[p], 0, [[8 * TILE, 128], [TILE, 8], [1, TILE]])
                op("gp", lambda g, _i=xin, _o=xo_bf: g.dma_start(_o, _i),
                   writes=x1_bf_bs, key="x1_bf")
                op("sp", lambda sy, _i=xin, _o=xo_f: sy.dma_start(_o, _i),
                   writes=x1_f_bs[p], key="x1_f%d" % p)

            def stage_A(ti):
                p = ti % 2
                ugp = ug[p]
                for jc in range(32):
                    wsl, wsl_b = w1ring[jc % 4]
                    wap = AP(w1e_stash, jc * 128,
                             [[32768, 128], [4096, 8], [1, 128]])
                    op("sp", lambda g, _w=wsl, _a=wap: g.dma_start(_w[:], _a),
                       reads=[w1e_b], writes=[wsl_b], key="w1r%d" % (jc % 4),
                       cost=1100)
                    bh, bh_b = bankC()
                    def m1(t, _j=jc, _b=bh, _w=wsl):
                        last = None
                        for dc in range(8):
                            last = t.matmul(
                                _b[:], _w[:, dc * 128:(dc + 1) * 128],
                                x1ch(dc), start=(dc == 0), stop=(dc == 7))
                        return last
                    op("pe", m1, reads=[wsl_b] + x1_bf_bs,
                       writes=[bh_b], cost=2600)
                    op("act", lambda a, _j=jc, _b=bh, _u=ugp: a.activation(
                        _u[:, _j * TILE:(_j + 1) * TILE], _b[:], AF.Gelu,
                        bias=gbias[:, _j:_j + 1]),
                       reads=[bh_b, gbias_b], writes=[ug_bs[p][jc]])
                if ti == 0:
                    tap("ug_t0", ugp[:], (128, 32 * TILE), BF16, ug_bs[p])

            def stage_B(ti, dhalf):
                p = ti % 2
                ugp = ug[p]
                obanks = psC[4:8]
                ob_store[(ti, dhalf)] = obanks
                chm = [None]
                for kc in range(32):
                    mr, mr_b = mring[kc % 3]
                    op(("gp" if kc % 2 == 0 else "sp"),
                       lambda g, _m=mr, _k=kc, _dh=dhalf: g.dma_start(
                        _m[:, 0:512], AP(m_stash, _k * 4 * 1024 + _dh * 512,
                                         [[1024, 4], [128 * 1024, 32], [1, 512]])),
                       writes=[mr_b],
                       key=("mring%d" if kc % 2 == 0 else "mrs%d") % (kc % 3),
                       cost=800)
                    def m2(t, _k=kc, _m=mr, _u=ugp, _dh=dhalf,
                           _obs=tuple(h for h, _ in obanks)):
                        last = None
                        for dj in range(4):
                            last = t.matmul(
                                _obs[dj][:],
                                _m[:, dj * 128:(dj + 1) * 128],
                                _u[:, _k * TILE:(_k + 1) * TILE],
                                start=(_k == 0), stop=(_k == 31))
                        return last
                    wr_banks = [obanks[dj][1] for dj in range(4)]
                    chm[0] = op("pe", m2,
                                reads=[mr_b, ug_bs[p][kc]],
                                writes=(wr_banks if kc == 31 else []),
                                deps=(sum([b.wdeps() for b in wr_banks], [])
                                      if kc == 0 else [chm[0]]), cost=1400)

            def stage_Cevac(ti, dhalf):
                p = ti % 2
                obanks = ob_store.pop((ti, dhalf))
                xf = x1_f[p]
                for dj in range(4):
                    dc = dhalf * 4 + dj
                    op("dve", lambda v, _d=dc, _ob=obanks[dj][0], _x=xf:
                       v.scalar_tensor_tensor(
                           _x[:, _d * TILE:(_d + 1) * TILE], _ob[:],
                           obias[:, _d:_d + 1],
                           _x[:, _d * TILE:(_d + 1) * TILE], ALU.add, ALU.add),
                       reads=[obanks[dj][1], obias_b, x1_f_bs[p][dc]],
                       writes=[x1_f_bs[p][dc]])

            def stage_C(ti):
                p = ti % 2
                t0 = ti * TILE
                xf = x1_f[p]
                xfch = lambda c, _x=xf: _x[:, c * TILE:(c + 1) * TILE]
                ln_B(xfch, x1_f_bs[p], xfch, x1_f_bs[p], out_fm, t0,
                     xf[:, 0:8 * TILE], xf[:, 0:8 * TILE])

            stage_L(0); stage_A(0)
            stage_L(1)
            stage_B(0, 0); stage_Cevac(0, 0); stage_B(0, 1); stage_Cevac(0, 1)
            stage_A(1); stage_C(0)
            stage_L(2)
            stage_B(1, 0); stage_Cevac(1, 0); stage_B(1, 1); stage_Cevac(1, 1)
            stage_A(2); stage_C(1)
            stage_L(3)
            stage_B(2, 0); stage_Cevac(2, 0); stage_B(2, 1); stage_Cevac(2, 1)
            stage_A(3); stage_C(2)
            stage_B(3, 0); stage_Cevac(3, 0); stage_B(3, 1); stage_Cevac(3, 1)
            stage_C(3)

        S.raw("sp", None,
              deps=[(k, v) for k, v in S.cnt.items() if k.startswith("st_out")])

        order, newval, ep_totals = S.finalize()
        sems = {}
        for k in S.cnt:
            sems[k] = top.enter_context(nc.semaphore("sem_" + k))

        with nc.Block() as block:
            def emit_for(e):
                def body(eng):
                    hw = {}
                    cur_ep = 0
                    for i in order[e]:
                        o = S.ops[i]
                        if o["ep"] != cur_ep:
                            for dk, dv in ep_totals[o["ep"] - 1].items():
                                if dv > hw.get(dk, 0):
                                    eng.wait_ge(sems[dk], dv)
                                    hw[dk] = dv
                            cur_ep = o["ep"]
                        waits = {}
                        for d in o["deps"]:
                            dk = S.ops[d]["key"]
                            dv = newval[d]
                            if dv > waits.get(dk, 0):
                                waits[dk] = dv
                        for dk, dv in waits.items():
                            if dv > hw.get(dk, 0):
                                eng.wait_ge(sems[dk], dv)
                                hw[dk] = dv
                        inst = o["fn"](eng)
                        inst.then_inc(sems[o["key"]], o["step"])
                    if S.final_raw is not None and e == S.final_raw[0]:
                        for d in S.final_raw[2]:
                            dk = S.ops[d]["key"]
                            dv = newval[d]
                            if dv > hw.get(dk, 0):
                                eng.wait_ge(sems[dk], dv)
                                hw[dk] = dv
                return body

            block.tensor(emit_for("pe"))
            block.scalar(emit_for("act"))
            block.vector(emit_for("dve"))
            block.sync(emit_for("sp"))
            block.gpsimd(emit_for("gp"))

    return nc, taps


def kernel(**inputs):
    in_maps = build_in_maps(inputs)
    nc, taps = build_nc(DEBUG_TAPS)
    res = run_bass_kernel_spmd(nc, in_maps, core_ids=list(range(8)))
    out = np.zeros((B, T, C), np.float32)
    for c in range(8):
        b, hf = c // 2, c % 2
        out[b, hf * 2048:(hf + 1) * 2048, :] = \
            np.asarray(res.results[c]["out_fm"], np.float32).T
    if DEBUG_TAPS:
        kernel.last_results = res
    return out
